# version 04
# baseline (speedup 1.0000x reference)
import numpy as np
import contextlib
import concourse.bass as bass
import concourse.mybir as mybir
import ml_dtypes

F32 = mybir.dt.float32
BF16 = mybir.dt.bfloat16
I32 = mybir.dt.int32
AF = mybir.ActivationFunctionType
ALU = mybir.AluOpType
AX = mybir.AxisListType
BFNP = ml_dtypes.bfloat16

H, Kd, C = 16, 64, 64
DM = 1024
HID = 4096
LN2 = float(np.log(2.0))
NEG_BIG = -30000.0
STAGE = 99
CHSTAGE = 99
NOQT = False
DBG = False
XMID_OUT = False
NO_BIAS_MM = False
FFN_DBG = False

V, S, T, G, SY = "vector", "scalar", "tensor", "gpsimd", "sync"
sT, sV, sS, sG, sL = "t", "v", "s", "g", "l"


class Plan:
    ENGINES = (SY, S, V, T, G)

    def __init__(self):
        self.q = {e: [] for e in self.ENGINES}
        self.sems = {}
        self.count = {}

    def op(self, eng, fn, inc=None, inc_n=1, drain=None):
        if drain is None:
            drain = eng in (V, S, G)
        if drain:
            self.q[eng].append(lambda e: e.drain())
        if inc is not None:
            self.count[inc] = self.count.get(inc, 0) + inc_n
            def wrapped(e, fn=fn, inc=inc, inc_n=inc_n):
                fn(e).then_inc(self.sems[inc], inc_n)
            self.q[eng].append(wrapped)
        else:
            self.q[eng].append(fn)

    def wait(self, eng, sem, val=None):
        v = self.count.get(sem, 0) if val is None else val
        self.q[eng].append(lambda e, sem=sem, v=v: e.wait_ge(self.sems[sem], v))

    def now(self, sem):
        return self.count.get(sem, 0)


class NS:
    pass


def make_consts(nc):
    c = NS()
    c.ident = nc.inline_tensor(np.eye(128, dtype=np.float32), name="ident_f32")
    i64stack = np.vstack([np.eye(64, dtype=np.float32)] * 2)
    c.i64f = nc.inline_tensor(i64stack, name="i64stackf")
    tt = np.arange(64)
    mA = np.where(tt[:, None] > tt[None, :], 0.0, NEG_BIG).astype(np.float32)
    mB = np.where(tt[:, None] < tt[None, :], 0.0, NEG_BIG).astype(np.float32)
    mQ = np.where(tt[:, None] <= tt[None, :], 0.0, NEG_BIG).astype(np.float32)
    c.maskA = nc.inline_tensor(np.vstack([mA, mA]), name="maskA")
    c.maskB = nc.inline_tensor(np.vstack([mB, mB]), name="maskB")
    c.maskQ = nc.inline_tensor(np.vstack([mQ, mQ]), name="maskQ")
    c.ones1x128 = nc.inline_tensor(np.ones((1, 128), np.float32), name="ones1x128")
    e16 = np.zeros((16, 5, 16), np.float32)
    for q in range(5):
        for h in range(16):
            e16[h, q, (h % 2) * 8 + h // 2] = 1.0
    c.qid = nc.inline_tensor(e16.reshape(16, 80), name="qid16")
    ind2 = np.zeros((16, 8, 128), np.float32)
    for a in range(8):
        ind2[2 * a, a, :64] = 1.0
        ind2[2 * a + 1, a, 64:] = 1.0
    c.ind2 = nc.inline_tensor(ind2.reshape(16, 8 * 128), name="ind2")
    gml = np.zeros((16, 128), np.float32)
    gml[0::2, :64] = 1.0
    gml[1::2, 64:] = 1.0
    c.gplace = nc.inline_tensor(gml, name="gplace")
    gmask = np.zeros((16, 8), np.float32)
    for h in range(16):
        gmask[h, h // 2] = 1.0
    c.gmask = nc.inline_tensor(gmask, name="gmask")
    indrep = np.zeros((16, 16, 128), np.float32)
    for h in range(16):
        indrep[h, h, :] = 1.0
    c.indrep = nc.inline_tensor(indrep.reshape(16, 16 * 128), name="indrep")
    c.identb = nc.inline_tensor(np.eye(128).astype(BFNP), name="identb")
    c.ones16 = nc.inline_tensor(np.ones((16, 1), np.float32), name="ones16")
    inds = np.zeros((128, 8, 16), np.float32)
    for a in range(8):
        inds[:64, a, 2 * a] = 1.0
        inds[64:, a, 2 * a + 1] = 1.0
    c.ind_ss = nc.inline_tensor(inds.reshape(128, 8 * 16).astype(np.float32), name="ind_ss")
    inde = np.zeros((16, 8, 128), np.float32)
    for a in range(8):
        inde[2 * a, a, :64] = 1.0
        inde[2 * a + 1, a, 64:] = 1.0
    c.ind_ex = nc.inline_tensor(inde.reshape(16, 8 * 128), name="ind_ex")
    c.cvals = [1e-5, -1027.0 * LN2, 1027.0 * LN2]
    c.cval_t = nc.inline_tensor(np.tile(np.array([c.cvals], np.float32), (128, 1)), name="cvals")
    return c




CONST_SPECS = [
    ("ident", [128, 128], F32), ("i64f", [128, 64], F32),
    ("maskA", [128, 64], F32), ("maskB", [128, 64], F32), ("maskQ", [128, 64], F32),
    ("ones1x128", [1, 128], F32), ("qid", [16, 80], F32), ("ind2", [16, 8 * 128], F32),
    ("gplace", [16, 128], F32), ("gmask", [16, 8], F32), ("indrep", [16, 16 * 128], F32), ("identb", [128, 128], BF16), ("ones16", [16, 1], F32), ("ind_ss", [128, 128], F32), ("ind_ex", [16, 1024], F32),
]


def stage_consts(nc, p, sb, st):
    for nm, shape, dt in CONST_SPECS:
        dram = getattr(sb.c, nm)
        tile = st.enter_context(nc.sbuf_tensor("c_" + nm, shape, dt))
        setattr(sb.c, nm + "_s", tile)
        p.op(SY, lambda e, tile=tile, dram=dram: e.dma_start(out=tile[:, :], in_=dram.ap()[:, :]), inc=sL, inc_n=16)
    cvt = st.enter_context(nc.sbuf_tensor("c_cvals", [128, len(sb.c.cvals)], F32))
    sb.c.cvals_s = cvt
    p.op(SY, lambda e, cvt=cvt: e.dma_start(out=cvt[:, :], in_=sb.c.cval_t.ap()[:, :]), inc=sL, inc_n=16)
    for i, v in enumerate(sb.c.cvals):
        nc.const_aps.register(F32, float(v), cvt[:, i:i + 1])


R_CRS = 0
R_CRT = 1024
R_GRAM = 2048
R_PA = 0
R_PB = 1024
R_QTP = 2048
R_X = 0
R_O = 1024
R_O2 = 2048
R_SS = 3072
R_QC = 0
R_GR = 3584


def emit_mixer_tile(p, sb, cn, tile, NCH, TT, is_dup):
    ps = sb.ps
    hcol = lambda h: slice(h * 64, (h + 1) * 64)

    for grp in range(NCH // 2):
        sl2 = slice(grp * 2 * C, (grp * 2 + 2) * C)
        base2 = [0, 64]
        p.wait(V, sS, None)
        for ci in range(2):
            csl = slice((grp * 2 + ci) * C, (grp * 2 + ci + 1) * C)
            last = (grp * 2 + ci + 1) * C - 1
            def f(e, csl=csl, last=last, ci=ci):
                return e.tensor_tensor(sb.t16[:, ci * C:(ci + 1) * C],
                                       sb.gcT[:, last:last + 1].broadcast_to([16, C]),
                                       sb.gcT[:, csl], ALU.subtract)
            p.op(V, f, inc=sV)
        p.wait(S, sV, None)
        p.wait(S, sT, None)
        p.op(S, lambda e: e.activation(sb.t16b[:, 0:128], sb.t16[:, 0:128], AF.Exp), inc=sS)
        p.wait(T, sV, None)
        p.wait(T, sS, None)
        qsrc = [(sb.gcT, sl2), (sb.nbpT, sl2), (sb.betaT, sl2), (sb.t16b, slice(0, 128)), (sb.gmbT, sl2)]
        for q, (ten, sl) in enumerate(qsrc):
            def f(e, q=q, ten=ten, sl=sl):
                return e.matmul(ps[:, R_QC + q * 16:R_QC + (q + 1) * 16],
                                ten[:, sl], sb.c.qid_s[:, q * 16:(q + 1) * 16],
                                start=True, stop=True)
            p.op(T, f, inc=sT)
        p.wait(V, sT, None)
        p.op(V, lambda e: e.tensor_copy(sb.COLS[:, :], ps[:, R_QC:R_QC + 80]), inc=sV)
        colsv = p.now(sV)

        if STAGE < 2:
            continue
        p.wait(T, sV, None)
        for ci in range(2):
            csl = slice((grp * 2 + ci) * C, (grp * 2 + ci + 1) * C)
            for s in range(H):
                h = (s % 8) * 2 + (s // 8)
                a, half = h // 2, h % 2
                def f(e, h=h, s=s, csl=csl, ci=ci):
                    return e.matmul(ps[ci * 64:(ci + 1) * 64, R_CRS + s * 64:R_CRS + (s + 1) * 64],
                                    sb.c.indrep_s[:, h * 128:h * 128 + 64], sb.gcT[:, csl],
                                    start=True, stop=True, tile_position=(0, ci * 64))
                p.op(T, f, inc=sT)
                def f(e, h=h, s=s, csl=csl, ci=ci):
                    return e.matmul(ps[ci * 64:(ci + 1) * 64, R_CRT + s * 64:R_CRT + (s + 1) * 64],
                                    sb.c.indrep_s[:, h * 128:h * 128 + 64], sb.gmbT[:, csl],
                                    start=True, stop=True, tile_position=(0, ci * 64))
                p.op(T, f, inc=sT)
                gcol = R_GRAM + s * 128
                if is_dup:
                    def f(e, a=a, half=half, csl=csl, ci=ci, gcol=gcol):
                        return e.matmul(ps[ci * 64:(ci + 1) * 64, gcol:gcol + 64],
                                        sb.kqT[half * 64:(half + 1) * 64, a, 0, csl],
                                        sb.kqT[half * 64:(half + 1) * 64, a, 0, csl],
                                        start=True, stop=True, tile_position=(half * 64, ci * 64))
                else:
                    def f(e, a=a, half=half, csl=csl, ci=ci, gcol=gcol):
                        return e.matmul(ps[ci * 64:(ci + 1) * 64, gcol:gcol + 128],
                                        sb.kqT[half * 64:(half + 1) * 64, a, 0, csl],
                                        sb.kqT[half * 64:(half + 1) * 64, a, :, csl],
                                        start=True, stop=True, tile_position=(half * 64, ci * 64))
                p.op(T, f, inc=sT)
        gramt = p.now(sT)

        if STAGE < 3:
            continue
        colb = lambda q, b0=0, n=128: sb.COLS[b0:b0 + n, q * 16:(q + 1) * 16].unsqueeze(2).broadcast_to([n, 16, 64])
        maskb = lambda mc: mc[:, :].unsqueeze(1).broadcast_to([128, 16, 64])
        st3 = lambda t: t[:, :].rearrange("p (h t) -> p h t", h=16)
        p.wait(V, sT, gramt)
        crs3 = ps[:, R_CRS:R_CRS + 1024].rearrange("p (h t) -> p h t", h=16)
        crt3 = ps[:, R_CRT:R_CRT + 1024].rearrange("p (h t) -> p h t", h=16)
        p.op(V, lambda e: e.scalar_tensor_tensor(st3(sb.strip1), crs3, -1.0, colb(4), ALU.mult, ALU.add), inc=sV)
        p.op(V, lambda e: e.tensor_tensor(st3(sb.strip1), st3(sb.strip1), maskb(sb.c.maskA_s), ALU.add), inc=sV)
        p.op(V, lambda e: e.scalar_tensor_tensor(st3(sb.strip2), colb(0), -1.0, crt3, ALU.mult, ALU.add), inc=sV)
        p.op(V, lambda e: e.tensor_tensor(st3(sb.strip2), st3(sb.strip2), maskb(sb.c.maskB_s), ALU.add), inc=sV)
        if not is_dup:
            p.op(V, lambda e: e.scalar_tensor_tensor(st3(sb.strip3), colb(0), -1.0, crs3, ALU.mult, ALU.add), inc=sV)
            p.op(V, lambda e: e.tensor_tensor(st3(sb.strip3), st3(sb.strip3), maskb(sb.c.maskQ_s), ALU.add), inc=sV)
        av = p.now(sV)
        p.wait(S, sV, av)
        for st in ([sb.strip1, sb.strip2] + ([sb.strip3] if not is_dup else [])):
            p.op(S, lambda e, st=st: e.activation(st[:, :], st[:, :], AF.Exp), inc=sS)
        p.wait(V, sS, None)
        g4 = ps[:, R_GRAM:R_GRAM + 2048].rearrange("p (par a b t) -> p par a b t", par=2, a=8, b=2)
        pw3 = lambda sl: sb.pw[:, sl, :, :].rearrange("p h t -> p h t")
        gg = ps[:, R_GRAM:R_GRAM + 2048].rearrange("p (s b t) -> p s b t", s=16, b=2)
        p.op(V, lambda e: e.tensor_tensor(sb.pw[:, 0, :, :], st3(sb.strip1), gg[:, :, 0, :], ALU.mult), inc=sV)
        p.op(V, lambda e: e.tensor_tensor(sb.pw[:, 1, :, :], st3(sb.strip2), gg[:, :, 0, :], ALU.mult), inc=sV)
        if not is_dup:
            p.op(V, lambda e: e.tensor_tensor(sb.AqkT[:, :, :], st3(sb.strip3), gg[:, :, 1, :], ALU.mult), inc=sV)
        abv = p.now(sV)

        if STAGE < 4:
            continue
        p.wait(T, sV, abv)
        steps = [(2, 1, 0, 2, 6), (4, 6, 2, 3, 7), (8, 7, 3, 4, 8), (16, 8, 4, 5, 9)]
        for (lv, bs, asl, outa, outb) in steps:
            for ci in range(2):
                b0 = ci * 64
                for h in range(H):
                    def f(e, bs=bs, asl=asl, h=h, b0=b0):
                        return e.matmul(ps[b0:b0 + 64, R_PA + h * 64:R_PA + (h + 1) * 64],
                                        sb.pw[b0:b0 + 64, bs, h, :], sb.pw[b0:b0 + 64, asl, h, :],
                                        start=True, stop=True, tile_position=(b0, b0))
                    p.op(T, f, inc=sT)
                    def f(e, bs=bs, asl=asl, h=h, b0=b0):
                        return e.matmul(ps[b0:b0 + 64, R_PB + h * 64:R_PB + (h + 1) * 64],
                                        sb.pw[b0:b0 + 64, asl, h, :], sb.pw[b0:b0 + 64, bs, h, :],
                                        start=True, stop=True, tile_position=(b0, b0))
                    p.op(T, f, inc=sT)
            tl = p.now(sT)
            p.wait(S, sT, tl)
            p.op(S, lambda e, outa=outa: e.copy(sb.pw[:, outa, :, :], ps[:, R_PA:R_PA + 1024].rearrange("p (h t) -> p h t", h=16)), inc=sS)
            p.op(S, lambda e, outb=outb: e.copy(sb.pw[:, outb, :, :], ps[:, R_PB:R_PB + 1024].rearrange("p (h t) -> p h t", h=16)), inc=sS)
            p.wait(V, sT, tl)
            bp = {2: 0, 4: 1, 8: 2, 16: 3}[lv]
            p.op(V, lambda e, bp=bp: e.tensor_tensor(
                sb.pwp[:, bp, :, :], ps[:, R_PB:R_PB + 1024].rearrange("p (h t) -> p h t", h=16),
                sb.c.i64f_s[:, :].unsqueeze(1).broadcast_to([128, 16, 64]), ALU.add), inc=sV)
            p.wait(T, sS, None)
            p.wait(T, sV, None)
        for ci in range(2):
            b0 = ci * 64
            for h in range(H):
                def f(e, h=h, b0=b0):
                    return e.matmul(ps[b0:b0 + 64, R_PB + h * 64:R_PB + (h + 1) * 64],
                                    sb.pw[b0:b0 + 64, 5, h, :], sb.pw[b0:b0 + 64, 9, h, :],
                                    start=True, stop=True, tile_position=(b0, b0))
                p.op(T, f, inc=sT)
        p.wait(V, sT, None)
        p.op(V, lambda e: e.tensor_tensor(
            sb.pwp[:, 4, :, :], ps[:, R_PB:R_PB + 1024].rearrange("p (h t) -> p h t", h=16),
            sb.c.i64f_s[:, :].unsqueeze(1).broadcast_to([128, 16, 64]), ALU.add), inc=sV)
        pwv = p.now(sV)

        if STAGE < 5:
            continue
        if (not is_dup) and STAGE >= 6:
            p.wait(S, sV, colsv)
            p.wait(S, sT, None)
            p.op(S, lambda e, sl2=sl2: e.activation(sb.t16b[:, 0:128], sb.gcT[:, sl2], AF.Exp), inc=sS)
            p.wait(T, sS, None)
            p.wait(T, sV, pwv)
            for a in range(8):
                def f(e, a=a):
                    return e.matmul(ps[:, R_QTP + a * 128:R_QTP + (a + 1) * 128],
                                    sb.c.ind2_s[:, a * 128:(a + 1) * 128], sb.t16b[:, 0:128],
                                    start=True, stop=True)
                p.op(T, f, inc=sT)
            p.wait(V, sT, None)
            for a in range(8):
                p.op(V, lambda e, a=a, sl2=sl2: e.tensor_tensor(sb.QtT[:, a, :], sb.kqT[:, a, 1, sl2],
                                                       ps[:, R_QTP + a * 128:R_QTP + (a + 1) * 128], ALU.mult), inc=sV)

        if STAGE < 7:
            continue
        for ci in range(2):
            b0 = ci * 64
            csl = slice((grp * 2 + ci) * C, (grp * 2 + ci + 1) * C)
            sub = ((grp * 2 + ci) * C) // 128
            def vsel(ten, par, b0=b0, sub=sub):
                return ten[b0:b0 + 64, sub, :].rearrange("p (a two t) -> p a two t", two=2, t=64)[:, :, par, :]
            def cbp(q, par, b0=b0):
                return sb.COLS[b0:b0 + 64, q * 16 + par * 8:q * 16 + par * 8 + 8].unsqueeze(2).broadcast_to([64, 8, 64])
            p.wait(V, sV, colsv)
            p.wait(V, sT, None)
            for par in range(2):
                p.op(V, lambda e, par=par, b0=b0, vsel=vsel, cbp=cbp: e.tensor_tensor(
                    st3h(sb.bv, b0)[:, par * 8:(par + 1) * 8, :], vsel(sb.vtok, par), cbp(2, par), ALU.mult), inc=sV)
                p.op(V, lambda e, par=par, b0=b0, vsel=vsel, cbp=cbp: e.tensor_tensor(
                    st3h(sb.kbar, b0)[:, par * 8:(par + 1) * 8, :], vsel(sb.ktok, par), cbp(3, par), ALU.mult), inc=sV)
            bvv = p.now(sV)
            if CHSTAGE < 2:
                continue
            p.wait(T, sV, bvv)
            for s in range(H):
                h = (s % 8) * 2 + (s // 8)
                a, half = h // 2, h % 2
                xc = R_X + s * 64
                def f(e, a=a, half=half, csl=csl, b0=b0, xc=xc):
                    return e.matmul(ps[b0:b0 + 64, xc:xc + 64],
                                    sb.kqT[half * 64:(half + 1) * 64, a, 0, csl],
                                    sb.Sbf[half * 64:(half + 1) * 64, a, :],
                                    start=True, stop=True, tile_position=(half * 64, b0))
                p.op(T, f, inc=sT)
            p.wait(V, sT, None)
            if CHSTAGE < 3:
                continue
            def f(e, b0=b0):
                return e.tensor_tensor(
                    st3h(sb.strip1, b0),
                    ps[b0:b0 + 64, R_X:R_X + 1024].rearrange("p (s t) -> p s t", s=16),
                    sb.COLS[b0:b0 + 64, 16:32].unsqueeze(2).broadcast_to([64, 16, 64]),
                    ALU.mult)
            p.op(V, f, inc=sV)
            def f(e, b0=b0):
                return e.tensor_tensor(sb.X[b0:b0 + 64, 0, :, :], st3h(sb.strip1, b0),
                                       st3h(sb.bv, b0), ALU.add)
            p.op(V, f, inc=sV)
            xv = p.now(sV)
            if CHSTAGE < 4:
                continue
            cur = 0
            for lv in [32, 16, 8, 4, 2]:
                bp = {32: 4, 16: 3, 8: 2, 4: 1, 2: 0}[lv]
                p.wait(T, sV, xv)
                for s in range(H):
                    xc = R_X + s * 64
                    def f(e, bp=bp, s=s, cur=cur, b0=b0, xc=xc):
                        return e.matmul(ps[b0:b0 + 64, xc:xc + 64],
                                        sb.pwp[b0:b0 + 64, bp, s, :], sb.X[b0:b0 + 64, cur, s, :],
                                        start=True, stop=True, tile_position=(b0, b0))
                    p.op(T, f, inc=sT)
                p.wait(V, sT, None)
                p.op(V, lambda e, cur=cur, b0=b0: e.tensor_copy(
                    sb.X[b0:b0 + 64, 1 - cur, :, :],
                    ps[b0:b0 + 64, R_X:R_X + 1024].rearrange("p (s t) -> p s t", s=16)), inc=sV)
                xv = p.now(sV)
                cur = 1 - cur
            p.wait(T, sV, xv)
            for s in range(H):
                xc = R_X + s * 64
                def f(e, s=s, cur=cur, b0=b0, xc=xc):
                    return e.matmul(ps[b0:b0 + 64, xc:xc + 64],
                                    sb.pw[b0:b0 + 64, 1, s, :], sb.X[b0:b0 + 64, cur, s, :],
                                    start=True, stop=True, tile_position=(b0, b0))
                p.op(T, f, inc=sT)
            p.wait(V, sT, None)
            p.op(V, lambda e, cur=cur, b0=b0: e.scalar_tensor_tensor(
                sb.u[b0:b0 + 64, :, :],
                ps[b0:b0 + 64, R_X:R_X + 1024].rearrange("p (s t) -> p s t", s=16), -1.0,
                sb.X[b0:b0 + 64, cur, :, :], ALU.mult, ALU.add), inc=sV)
            uv = p.now(sV)
            if CHSTAGE < 6:
                continue
            if not is_dup:
                p.wait(T, sV, uv)
                for s in range(H):
                    oc = R_O + s * 64
                    def f(e, s=s, b0=b0, oc=oc):
                        return e.matmul(ps[0:64, oc:oc + 64],
                                        sb.u[b0:b0 + 64, s, :], sb.AqkT[b0:b0 + 64, s, :],
                                        start=True, stop=True, tile_position=(b0, 0))
                    p.op(T, f, inc=sT)
                for s in (range(H) if not NOQT else []):
                    h = (s % 8) * 2 + (s // 8)
                    a, half = h // 2, h % 2
                    oc = R_O2 + s * 64
                    def f(e, a=a, half=half, ci=ci, oc=oc):
                        return e.matmul(ps[0:64, oc:oc + 64],
                                        sb.Sbf[half * 64:(half + 1) * 64, a, :],
                                        sb.QtT[half * 64:(half + 1) * 64, a, ci * 64:(ci + 1) * 64],
                                        start=True, stop=True, tile_position=(half * 64, 0))
                    p.op(T, f, inc=sT)
                ovt = p.now(sT)
                p.wait(V, sT, ovt)
                o1 = ps[0:64, R_O:R_O + 1024].rearrange("p (par a t) -> p par a t", par=2, a=8)
                o2 = ps[0:64, R_O2:R_O2 + 1024].rearrange("p (par a t) -> p par a t", par=2, a=8)
                p.op(V, lambda e, csl=csl, o1=o1: e.tensor_copy(sb.oT[0:64, :, csl], o1[:, 0, :, :]), inc=sV)
                p.op(V, lambda e, csl=csl, o1=o1: e.tensor_copy(sb.oT[64:128, :, csl], o1[:, 1, :, :]), inc=sV)
                if not NOQT:
                    p.op(V, lambda e, csl=csl, o2=o2: e.tensor_tensor(sb.oT[0:64, :, csl], sb.oT[0:64, :, csl], o2[:, 0, :, :], ALU.add), inc=sV)
                    p.op(V, lambda e, csl=csl, o2=o2: e.tensor_tensor(sb.oT[64:128, :, csl], sb.oT[64:128, :, csl], o2[:, 1, :, :], ALU.add), inc=sV)
            if CHSTAGE < 7:
                continue
            p.wait(T, sV, uv)
            for s in range(H):
                h = (s % 8) * 2 + (s // 8)
                a, half = h // 2, h % 2
                def f(e, s=s, a=a, half=half, b0=b0):
                    return e.matmul(ps[half * 64:(half + 1) * 64, R_SS + a * 64:R_SS + (a + 1) * 64],
                                    sb.kbar[b0:b0 + 64, s * 64:(s + 1) * 64],
                                    sb.u[b0:b0 + 64, s, :],
                                    start=True, stop=True, tile_position=(b0, half * 64))
                p.op(T, f, inc=sT)
            svt = p.now(sT)
            last = (grp * 2 + ci + 1) * C - 1
            p.wait(S, sV, None)
            p.wait(S, sT, None)
            p.op(S, lambda e, last=last: e.activation(sb.t16[:, 0:1], sb.gcT[:, last:last + 1], AF.Exp), inc=sS)
            p.wait(V, sS, None)
            p.op(V, lambda e: e.tensor_tensor(sb.t16b[:, 0:8], sb.t16[:, 0:1].broadcast_to([16, 8]),
                                              sb.c.gmask_s[:, :], ALU.mult), inc=sV)
            p.wait(T, sV, None)
            p.op(T, lambda e: e.matmul(ps[:, R_GR:R_GR + 8], sb.c.gplace_s[:, :], sb.t16b[:, 0:8],
                                       start=True, stop=True), inc=sT)
            p.wait(V, sT, None)
            p.op(V, lambda e: e.tensor_tensor(
                sb.S32[:, :, :], sb.S32[:, :, :],
                ps[:, R_GR:R_GR + 8].unsqueeze(2).broadcast_to([128, 8, 64]), ALU.mult), inc=sV)
            p.op(V, lambda e: e.tensor_tensor(
                sb.S32[:, :, :], sb.S32[:, :, :],
                ps[:, R_SS:R_SS + 512].rearrange("p (a t) -> p a t", a=8), ALU.add), inc=sV)
            p.op(V, lambda e: e.tensor_copy(sb.Sbf[:, :, :], sb.S32[:, :, :]), inc=sV)


def st3h(ten, b0):
    return ten[b0:b0 + 64, :].rearrange("p (h t) -> p h t", h=16)


def emit_attn_front(p, sb, d, tile, TT, is_dup, is_first, mask_boundary, last_dup=False):
    NSUB = TT // 128
    ps = sb.ps

    for sub in range(NSUB):
        p.wait(S, sL, None)
        p.wait(S, sV, None)
        p.op(S, lambda e, sub=sub: e.activation(sb.scr1[:, :], sb.x[:, sub, :], AF.Square,
                                                accum_out=sb.st4[:, 0:1]), inc=sS)
        p.op(S, lambda e, sub=sub: e.activation(sb.scr1[:, :], sb.x[:, sub, :], AF.Identity,
                                                accum_out=sb.st4[:, 1:2]), inc=sS)
        p.wait(V, sS, None)
        p.op(V, lambda e: e.tensor_scalar(sb.st4[:, 2:3], sb.st4[:, 1:2], 1.0 / DM, None, ALU.mult), inc=sV)
        p.op(V, lambda e: e.tensor_scalar(sb.st4[:, 3:4], sb.st4[:, 2:3], sb.st4[:, 2:3], None, ALU.mult), inc=sV)
        p.op(V, lambda e: e.scalar_tensor_tensor(sb.st4[:, 3:4], sb.st4[:, 0:1], 1.0 / DM,
                                                 sb.st4[:, 3:4], ALU.mult, ALU.subtract), inc=sV)
        p.wait(S, sV, None)
        p.op(S, lambda e: e.activation(sb.st4[:, 3:4], sb.st4[:, 3:4], AF.Ln, bias=1e-5), inc=sS)
        p.op(S, lambda e: e.activation(sb.st4[:, 3:4], sb.st4[:, 3:4], AF.Exp, scale=-0.5), inc=sS)
        p.wait(V, sS, None)
        p.op(V, lambda e, sub=sub: e.tensor_scalar(sb.xn_bf[:, :], sb.x[:, sub, :],
                                                   sb.st4[:, 2:3], sb.st4[:, 3:4],
                                                   ALU.subtract, ALU.mult), inc=sV)
        xnv = p.now(sV)
        p.wait(T, sV, xnv)
        p.wait(T, sS, None)
        for dc in range(8):
            bank = (dc % 4) * 128
            p.op(T, lambda e, dc=dc, bank=bank: e.matmul(ps[:, 3072 + bank:3072 + bank + 128],
                                                            lhsT=sb.xn_bf[:, dc * 128:(dc + 1) * 128], rhs=sb.c.identb_s[:, :], start=True, stop=True), inc=sT)
            if dc % 4 == 3:
                p.wait(V, sT, None)
                p.op(V, lambda e, dc=dc, sub=sub: e.tensor_copy(
                    sb.xnT[:, dc - 3:dc + 1, sub * 128:(sub + 1) * 128],
                    ps[:, 3072:3072 + 512].rearrange("p (a t) -> p a t", a=4)), inc=sV)
                p.wait(T, sV, None)

    projs = [("k", sb.wv_buf, 0), ("v", sb.wv_buf, 1)] + ([("q", sb.wv_buf, 2)] if (not is_dup or last_dup) else [])
    for (pname, wsb, pj) in projs:
        if pname != "k":
            p.wait(SY, sT, None)
            wsrc = d.wvq if pname == "v" else d.wqq
            p.op(SY, lambda e, wsrc=wsrc: e.dma_start(out=sb.wv_buf[:, :, :],
                                                      in_=wsrc.rearrange("(a p) c -> p a c", p=128)), inc=sL, inc_n=16)
        p.wait(T, sL, None)
        for oc in range(8):
            bank = (oc % 2) * 512
            p.wait(T, sS, None)
            p.wait(T, sV, None)
            p.wait(T, sG, None)
            for cc in range(8):
                def f(e, oc=oc, cc=cc, bank=bank, wsb=wsb):
                    return e.matmul(ps[:, bank:bank + 512],
                                    wsb[:, cc, oc * 128:(oc + 1) * 128],
                                    sb.xnT[:, cc, :],
                                    start=(cc == 0), stop=(cc == 7))
                p.op(T, f, inc=sT)
            p.wait(S, sT, None)
            p.wait(S, sV, None)
            if is_first or (pname == "q" and is_dup):
                p.op(S, lambda e: e.memzero(sb.raw_oc[:, 0:3]), inc=sS)
            else:
                p.op(S, lambda e, pj=pj, oc=oc: e.copy(sb.raw_oc[:, 0:3], sb.halo[:, pj, oc, :]), inc=sS)
            p.op(S, lambda e, oc=oc, bank=bank: e.copy(sb.raw_oc[:, 3:TT + 3], ps[:, bank:bank + 512]), inc=sS)
            p.wait(V, sS, None)
            if mask_boundary and not is_first:
                p.op(V, lambda e: e.tensor_tensor(sb.raw_oc[:, 0:3], sb.raw_oc[:, 0:3],
                                                  sb.maskrep[:, 0:1].broadcast_to([128, 3]), ALU.mult), inc=sV)
            p.op(V, lambda e, pj=pj, oc=oc: e.tensor_copy(sb.halo[:, pj, oc, :], sb.raw_oc[:, TT:TT + 3]), inc=sV)
            wcol = lambda j, pj=pj, oc=oc: sb.convw_sb[:, oc, pj * 4 + j:pj * 4 + j + 1]
            bcol = lambda pj=pj, oc=oc: sb.convb_sb[:, oc, pj:pj + 1]
            p.wait(S, sV, None)
            p.op(S, lambda e, wcol=wcol, bcol=bcol: e.activation(
                sb.cv1[:, :], sb.raw_oc[:, 0:TT], AF.Identity, bias=bcol(), scale=wcol(0)), inc=sS)
            p.wait(V, sS, None)
            p.wait(V, sG, None)
            p.op(V, lambda e, wcol=wcol: e.scalar_tensor_tensor(
                sb.cv2[:, :], sb.raw_oc[:, 1:TT + 1], wcol(1), sb.cv1[:, :], ALU.mult, ALU.add), inc=sV)
            p.op(V, lambda e, wcol=wcol: e.scalar_tensor_tensor(
                sb.cv1[:, :], sb.raw_oc[:, 2:TT + 2], wcol(2), sb.cv2[:, :], ALU.mult, ALU.add), inc=sV)
            if pname == "k":
                dst = lambda oc=oc: sb.kqT[:, oc, 0, :]
            elif pname == "v":
                dst = lambda oc=oc: sb.cvv[:, :]
            else:
                dst = lambda oc=oc: sb.kqT[:, oc, 1, :]
            p.op(V, lambda e, wcol=wcol, dst=dst, oc=oc: e.scalar_tensor_tensor(
                dst(), sb.raw_oc[:, 3:TT + 3], wcol(3), sb.cv1[:, :], ALU.mult, ALU.add), inc=sV)
            if pname == "v":
                cvv_v = p.now(sV)
                p.wait(T, sV, cvv_v)
                for sub in range(NSUB):
                    bank2 = 2048 + (sub % 4) * 128
                    p.op(T, lambda e, sub=sub, bank2=bank2: e.matmul(ps[:, bank2:bank2 + 128], lhsT=sb.cvv[:, sub * 128:(sub + 1) * 128], rhs=sb.c.identb_s[:, :], start=True, stop=True), inc=sT)
                    if sub % 4 == 3 or sub == NSUB - 1:
                        nb = (sub % 4) + 1
                        p.wait(V, sT, None)
                        p.op(V, lambda e, oc=oc, sub=sub, nb=nb: e.tensor_copy(
                            sb.vtok[:, sub - nb + 1:sub + 1, oc * 128:(oc + 1) * 128].transpose([0, 2, 1]) if False else
                            sb.vtok[:, sub - nb + 1:sub + 1, oc * 128:(oc + 1) * 128],
                            ps[:, 2048:2048 + nb * 128].rearrange("p (a t) -> p a t", a=nb)), inc=sV)
                        p.wait(T, sV, None)

    p.wait(T, sV, None)
    p.wait(T, sS, None)
    for cc in range(8):
        p.op(T, lambda e, cc=cc: e.matmul(ps[0:48, 2048:2048 + TT],
                                          sb.wgb_sb[:, cc, :], sb.xnT[:, cc, :],
                                          start=(cc == 0), stop=(cc == 7)), inc=sT)
    p.wait(S, sT, None)
    p.op(S, lambda e: e.activation(sb.t16c[:, :], ps[32:48, 2048:2048 + TT], AF.Exp, scale=-1.0), inc=sS)
    p.op(S, lambda e: e.activation(sb.t16d[:, :], ps[0:16, 2048:2048 + TT], AF.Exp, scale=-1.0), inc=sS)
    p.wait(V, sS, None)
    p.op(V, lambda e: e.tensor_scalar(sb.t16c[:, :], sb.t16c[:, :], 1.0, None, ALU.add), inc=sV)
    p.op(V, lambda e: e.reciprocal(sb.betaT[:, :], sb.t16c[:, :]), inc=sV)
    p.op(V, lambda e: e.tensor_scalar(sb.t16d[:, :], sb.t16d[:, :], 1.0, None, ALU.add), inc=sV)
    p.op(V, lambda e: e.reciprocal(sb.t16d[:, :], sb.t16d[:, :]), inc=sV)
    p.wait(S, sV, None)
    p.op(S, lambda e: e.activation(sb.gT[:, :], sb.t16d[:, :], AF.Ln), inc=sS)
    p.op(S, lambda e: e.activation(sb.t16c[:, :], sb.betaT[:, :], AF.Ln), inc=sS)
    p.wait(V, sS, None)
    for ch in range(TT // C):
        p.op(V, lambda e, ch=ch: e.tensor_tensor_scan(
            sb.gcT[:, ch * C:(ch + 1) * C],
            sb.c.ones16_s[:, 0:1].broadcast_to([16, C]),
            sb.gT[:, ch * C:(ch + 1) * C], 0.0, ALU.mult, ALU.add), inc=sV)
    p.op(V, lambda e: e.tensor_tensor(sb.t16d[:, :], sb.gcT[:, :], sb.gT[:, :], ALU.subtract), inc=sV)
    p.op(V, lambda e: e.tensor_tensor(sb.gmbT[:, :], sb.t16d[:, :], sb.t16c[:, :], ALU.add), inc=sV)
    p.wait(S, sV, None)
    p.op(S, lambda e: e.activation(sb.t16d[:, :], sb.t16d[:, :], AF.Exp), inc=sS)
    p.wait(V, sS, None)
    p.op(V, lambda e: e.scalar_tensor_tensor(sb.nbpT[:, :], sb.t16d[:, :], -1.0, sb.betaT[:, :],
                                             ALU.mult, ALU.mult), inc=sV)

    for oc in range(8):
        p.wait(S, sV, None)
        if oc > 0:
            p.wait(S, sT, None)
        p.op(S, lambda e, oc=oc: e.activation(sb.cv1[:, :], sb.kqT[:, oc, 0, :], AF.Square), inc=sS)
        p.wait(T, sS, None)
        p.op(T, lambda e, oc=oc: e.matmul(ps[0:16, 3072:3072 + TT],
                                          sb.c.ind_ss_s[:, oc * 16:(oc + 1) * 16],
                                          sb.cv1[:, 0:TT],
                                          start=(oc == 0), stop=(oc == 7)), inc=sT)
    p.wait(V, sT, None)
    p.op(V, lambda e: e.tensor_scalar(sb.t16c[:, :], ps[0:16, 3072:3072 + TT], 1e-24, None, ALU.max), inc=sV)
    p.wait(S, sV, None)
    p.op(S, lambda e: e.activation(sb.t16c[:, :], sb.t16c[:, :], AF.Ln), inc=sS)
    p.op(S, lambda e: e.activation(sb.t16c[:, :], sb.t16c[:, :], AF.Exp, scale=-0.5), inc=sS)
    p.wait(V, sS, None)
    p.wait(T, sV, None)
    p.wait(T, sS, None)
    for oc in range(8):
        bank = (oc % 2) * 512
        p.op(T, lambda e, oc=oc, bank=bank: e.matmul(ps[:, bank:bank + TT],
                                                     sb.c.ind_ex_s[:, oc * 128:(oc + 1) * 128], sb.t16c[:, :],
                                                     start=True, stop=True), inc=sT)
        p.wait(V, sT, None)
        p.op(V, lambda e, oc=oc, bank=bank: e.tensor_tensor(sb.kqT[:, oc, 0, :], sb.kqT[:, oc, 0, :],
                                                            ps[:, bank:bank + TT], ALU.mult), inc=sV)
        p.wait(T, sV, None)
        for sub in range(NSUB):
            bank2 = 2048 + (sub % 4) * 128
            p.op(T, lambda e, oc=oc, sub=sub, bank2=bank2: e.matmul(ps[:, bank2:bank2 + 128], lhsT=sb.kqT[:, oc, 0, sub * 128:(sub + 1) * 128], rhs=sb.c.identb_s[:, :], start=True, stop=True), inc=sT)
            if sub % 4 == 3 or sub == NSUB - 1:
                nb = (sub % 4) + 1
                p.wait(V, sT, None)
                p.op(V, lambda e, oc=oc, sub=sub, nb=nb: e.tensor_copy(
                    sb.ktok[:, sub - nb + 1:sub + 1, oc * 128:(oc + 1) * 128],
                    ps[:, 2048:2048 + nb * 128].rearrange("p (a t) -> p a t", a=nb)), inc=sV)
                p.wait(T, sV, None)


def emit_quant(p, sb, src_fn, dst_fn, NSL, name=""):
    for i in range(NSL):
        p.wait(V, sS, None)
        p.op(V, lambda e, i=i: e.tensor_reduce(sb.qs[:, 0:16], src_fn(i).rearrange("p (n k) -> p n k", k=32),
                                               AX.X, ALU.max, apply_absolute_value=True), inc=sV)
        p.op(V, lambda e: e.tensor_scalar(sb.qs[:, 0:16], sb.qs[:, 0:16], 2.0 ** -100, None, ALU.max), inc=sV)
        p.wait(S, sV, None)
        p.op(S, lambda e: e.activation(sb.qs[:, 16:32], sb.qs[:, 0:16], AF.Ln), inc=sS)
        p.wait(V, sS, None)
        p.op(V, lambda e: e.tensor_scalar(sb.qs[:, 16:32], sb.qs[:, 16:32], 1.0 / LN2, 1024.0, ALU.mult, ALU.add), inc=sV)
        p.op(V, lambda e: e.tensor_copy(sb.qi[:, 0:16], sb.qs[:, 16:32]), inc=sV)
        p.op(V, lambda e: e.tensor_tensor(sb.qs[:, 32:48], sb.qi[:, 0:16], sb.qs[:, 16:32], ALU.is_gt), inc=sV)
        p.op(V, lambda e: e.tensor_tensor(sb.qs[:, 32:48], sb.qi[:, 0:16], sb.qs[:, 32:48], ALU.subtract), inc=sV)
        p.op(V, lambda e: e.tensor_copy(sb.qi[:, 0:16], sb.qs[:, 32:48]), inc=sV)
        p.op(V, lambda e: e.tensor_scalar(sb.qi[:, 16:32], sb.qi[:, 0:16], -900, None, ALU.add), inc=sV)
        p.op(V, lambda e: e.tensor_scalar(sb.qi[:, 16:32], sb.qi[:, 16:32], 23, None,
                                          ALU.logical_shift_left), inc=sV)
        p.op(V, lambda e: e.tensor_scalar(sb.qi[:, 32:48], sb.qi[:, 0:16], -1, 1154,
                                          ALU.mult, ALU.add), inc=sV)
        p.op(V, lambda e: e.tensor_scalar(sb.qi[:, 32:48], sb.qi[:, 32:48], 23, None,
                                          ALU.logical_shift_left), inc=sV)
        isb = sb.qi[:, 32:48].bitcast(F32).unsqueeze(2).broadcast_to([128, 16, 32])
        ssb = sb.qi[:, 16:32].bitcast(F32).unsqueeze(2).broadcast_to([128, 16, 32])
        m3 = sb.qm[:, :].rearrange("p (n k) -> p n k", k=32)
        p.op(V, lambda e, i=i, isb=isb, m3=m3: e.tensor_tensor(m3, src_fn(i).rearrange("p (n k) -> p n k", k=32), isb, ALU.mult), inc=sV)
        p.op(V, lambda e: e.tensor_scalar(sb.qr[:, :], sb.qm[:, :], 16.5, None, ALU.add), inc=sV)
        p.op(V, lambda e: e.tensor_scalar(sb.qd[:, :], sb.qr[:, :], -16.5, None, ALU.add), inc=sV)
        p.op(V, lambda e: e.tensor_tensor(sb.qf[:, :], sb.qd[:, :], sb.qm[:, :], ALU.is_gt), inc=sV)
        p.op(V, lambda e: e.tensor_tensor(sb.qd[:, :], sb.qd[:, :], sb.qf[:, :], ALU.subtract), inc=sV)
        d3 = sb.qd[:, :].rearrange("p (n k) -> p n k", k=32)
        p.op(V, lambda e, i=i, ssb=ssb, d3=d3: e.scalar_tensor_tensor(
            dst_fn(i).rearrange("p (n k) -> p n k", k=32), d3, 0.5, ssb, ALU.add, ALU.mult), inc=sV)


def emit_oproj(p, sb, tile, TT):
    NSUB = TT // 128
    ps = sb.ps
    p.wait(V, "st", None)
    for sub in range(NSUB):
        base = (sub % 2) * 2048
        p.wait(T, sV, None)
        p.wait(T, sS, None)
        p.wait(T, sL, None)
        for half in range(2):
            for cc in range(8):
                def f(e, sub=sub, half=half, cc=cc, base=base):
                    return e.matmul(ps[:, base + half * 512:base + (half + 1) * 512],
                                    sb.oT[:, cc, sub * 128:(sub + 1) * 128],
                                    sb.wv_buf[:, cc, half * 512:(half + 1) * 512],
                                    start=(cc == 0), stop=(cc == 7))
                p.op(T, f, inc=sT)
        p.wait(S, sT, None)
        p.wait(S, sV, None)
        p.op(S, lambda e, base=base: e.activation(sb.scr1[:, :], ps[:, base:base + 1024], AF.Square,
                                                  accum_out=sb.st4[:, 0:1]), inc=sS)
        p.op(S, lambda e, base=base: e.activation(sb.scr1[:, :], ps[:, base:base + 1024], AF.Identity,
                                                  accum_out=sb.st4[:, 1:2]), inc=sS)
        p.wait(V, sS, None)
        p.op(V, lambda e: e.tensor_scalar(sb.st4[:, 2:3], sb.st4[:, 1:2], 1.0 / DM, None, ALU.mult), inc=sV)
        p.op(V, lambda e: e.tensor_scalar(sb.st4[:, 3:4], sb.st4[:, 2:3], sb.st4[:, 2:3], None, ALU.mult), inc=sV)
        p.op(V, lambda e: e.scalar_tensor_tensor(sb.st4[:, 3:4], sb.st4[:, 0:1], 1.0 / DM,
                                                 sb.st4[:, 3:4], ALU.mult, ALU.subtract), inc=sV)
        p.wait(S, sV, None)
        p.op(S, lambda e: e.activation(sb.st4[:, 3:4], sb.st4[:, 3:4], AF.Ln, bias=1e-5), inc=sS)
        p.op(S, lambda e: e.activation(sb.st4[:, 3:4], sb.st4[:, 3:4], AF.Exp, scale=-0.5), inc=sS)
        p.wait(V, sS, None)
        p.op(V, lambda e, base=base: e.tensor_scalar(sb.scr1[:, :], ps[:, base:base + 1024],
                                                     sb.st4[:, 2:3], sb.st4[:, 3:4],
                                                     ALU.subtract, ALU.mult), inc=sV)
        p.op(V, lambda e, sub=sub: e.tensor_tensor(sb.xm[:, sub, :], sb.scr1[:, :], sb.x[:, sub, :], ALU.add), inc=sV)


def emit_ffn_tile(p, sb, d, tile, TT, sWF):
    NSUB = TT // 128
    ps = sb.ps

    def ln_pass(src_ap_fn, dst_fn, out_f32):
        p.wait(S, sL, None)
        p.wait(S, sV, None)
        p.op(S, lambda e: e.activation(sb.scr1[:, :], src_ap_fn(), AF.Square, accum_out=sb.st4[:, 0:1]), inc=sS)
        p.op(S, lambda e: e.activation(sb.scr1[:, :], src_ap_fn(), AF.Identity, accum_out=sb.st4[:, 1:2]), inc=sS)
        p.wait(V, sS, None)
        p.op(V, lambda e: e.tensor_scalar(sb.st4[:, 2:3], sb.st4[:, 1:2], 1.0 / DM, None, ALU.mult), inc=sV)
        p.op(V, lambda e: e.tensor_scalar(sb.st4[:, 3:4], sb.st4[:, 2:3], sb.st4[:, 2:3], None, ALU.mult), inc=sV)
        p.op(V, lambda e: e.scalar_tensor_tensor(sb.st4[:, 3:4], sb.st4[:, 0:1], 1.0 / DM,
                                                 sb.st4[:, 3:4], ALU.mult, ALU.subtract), inc=sV)
        p.wait(S, sV, None)
        p.op(S, lambda e: e.activation(sb.st4[:, 3:4], sb.st4[:, 3:4], AF.Ln, bias=1e-5), inc=sS)
        p.op(S, lambda e: e.activation(sb.st4[:, 3:4], sb.st4[:, 3:4], AF.Exp, scale=-0.5), inc=sS)
        p.wait(V, sS, None)
        p.op(V, lambda e: e.tensor_scalar(dst_fn(), src_ap_fn(), sb.st4[:, 2:3], sb.st4[:, 3:4],
                                          ALU.subtract, ALU.mult), inc=sV)

    for sub in range(NSUB):
        ln_pass(lambda sub=sub: sb.xm[:, sub, :], lambda: sb.xn2[:, :], True)
        emit_quant(p, sb, lambda i: sb.xn2[:, i * 512:(i + 1) * 512],
                   lambda i: sb.xq[:, i * 512:(i + 1) * 512], 2)
        if DBG or FFN_DBG:
            for nm, h in (("xn2_s%d" % sub, sb.xn2), ("xq_s%d" % sub, sb.xq)):
                t_ = sb.nc.dram_tensor("dbg_%s_t%d" % (nm, tile), [128, 1024], h.dtype, kind="ExternalOutput").ap()
                p.wait(SY, sV, None)
                p.op(SY, lambda e, t_=t_, h=h: e.dma_start(out=t_[:, :], in_=h[:, :]), inc="st", inc_n=16)
            p.wait(V, "st", None)
            p.wait(S, "st", None)
        p.wait(T, sV, None)
        p.wait(T, sS, None)
        for dc in range(8):
            bank = (dc % 4) * 128
            p.op(T, lambda e, dc=dc, bank=bank: e.matmul(ps[:, 2048 + bank:2048 + bank + 128],
                                                            lhsT=sb.xq[:, dc * 128:(dc + 1) * 128], rhs=sb.c.identb_s[:, :], start=True, stop=True), inc=sT)
            if dc % 4 == 3:
                p.wait(V, sT, None)
                p.op(V, lambda e, dc=dc, sub=sub: e.tensor_copy(
                    sb.xqT[:, dc - 3:dc + 1, sub * 128:(sub + 1) * 128],
                    ps[:, 2048:2048 + 512].rearrange("p (a t) -> p a t", a=4)), inc=sV)
                p.wait(T, sV, None)

    p.wait(SY, sT, None)
    p.op(SY, lambda e: e.dma_start(out=sb.bstg[:, :], in_=d.b_fc[:, :]), inc="sb", inc_n=16)
    p.wait(T, "sb", None)
    p.wait(T, sV, None)
    p.wait(T, sS, None)
    p.wait(T, sL, None)
    p.wait(T, "st", None)
    for hc in range(8):
        p.wait(SY, sT, None)
        p.op(SY, lambda e, hc=hc: e.dma_start(
            out=sb.wfc_buf[:, :, :],
            in_=d.wfcq.rearrange("(a p) hh -> p a hh", p=128)[:, :, hc * 512:(hc + 1) * 512]), inc=sWF, inc_n=16)
        p.wait(T, sWF, None)
        for sub in range(NSUB):
            bank = ((hc * NSUB + sub) % 4) * 512
            p.wait(T, sS, None)
            for cc in range(8):
                def f(e, hc=hc, sub=sub, cc=cc, bank=bank):
                    return e.matmul(ps[:, bank:bank + 512],
                                    sb.xqT[:, cc, sub * 128:(sub + 1) * 128],
                                    sb.wfc_buf[:, cc, :],
                                    start=(cc == 0), stop=(NO_BIAS_MM and cc == 7))
                p.op(T, f, inc=sT)
            if not NO_BIAS_MM:
                p.op(T, lambda e, hc=hc, bank=bank: e.matmul(
                    ps[:, bank:bank + 512], sb.c.ones1x128_s[:, :],
                    sb.bstg[:, hc * 512:(hc + 1) * 512], start=False, stop=True), inc=sT)
            p.wait(S, sT, None)
            p.op(S, lambda e, bank=bank: e.activation(sb.h32[:, :], ps[:, bank:bank + 512], AF.Gelu), inc=sS)
            emit_quant(p, sb, lambda i: sb.h32[:, :], lambda i: sb.hqt_tmp[:, :], 1)
            p.wait(T, sV, None)
            for j in range(4):
                p.op(T, lambda e, j=j: e.matmul(ps[:, 2048 + j * 128:2048 + (j + 1) * 128],
                                                   lhsT=sb.hqt_tmp[:, j * 128:(j + 1) * 128], rhs=sb.c.identb_s[:, :], start=True, stop=True), inc=sT)
            p.wait(V, sT, None)
            p.op(V, lambda e, hc=hc, sub=sub: e.tensor_copy(
                sb.hqT[:, sub, hc * 4:hc * 4 + 4, :],
                ps[:, 2048:2048 + 512].rearrange("p (a t) -> p a t", a=4)), inc=sV)
            p.wait(T, sV, None)

    if FFN_DBG:
        t_ = sb.nc.dram_tensor("dbg_hqT_t%d" % tile, [128, NSUB * 32 * 128], BF16, kind="ExternalOutput").ap()
        p.wait(SY, sV, None)
        p.op(SY, lambda e, t_=t_: e.dma_start(
            out=t_[:, :], in_=sb.hqT[:, :, :, :].rearrange("p a b t -> p (a b t)")), inc="st", inc_n=16)
        p.wait(T, "st", None)
        p.wait(V, "st", None)
        p.wait(S, "st", None)
    p.wait(SY, sT, None)
    p.op(SY, lambda e: e.dma_start(out=sb.bstg[:, 0:DM], in_=d.b_proj[:, :]), inc="sb", inc_n=16)
    p.wait(T, sS, None)
    p.wait(T, sV, None)
    for ccg in range(4):
        p.wait(SY, sT, None)
        p.op(SY, lambda e, ccg=ccg: e.dma_start(
            out=sb.wpr_buf[:, :, :],
            in_=d.wprq.rearrange("(a p) dd -> p a dd", p=128)[:, ccg * 8:(ccg + 1) * 8, :]), inc=sWF, inc_n=16)
        p.wait(T, sWF, None)
        if ccg == 3:
            p.wait(T, "sb", None)
        if ccg > 0:
            p.wait(T, sV, None)
        for sub in range(NSUB):
            for half in range(2):
                for c2 in range(8):
                    def f(e, sub=sub, c2=c2, ccg=ccg, half=half):
                        return e.matmul(ps[:, sub * 1024 + half * 512:sub * 1024 + (half + 1) * 512],
                                        sb.hqT[:, sub, ccg * 8 + c2, :],
                                        sb.wpr_buf[:, c2, half * 512:(half + 1) * 512],
                                        start=(c2 == 0), stop=(c2 == 7 and ccg != 3))
                    p.op(T, f, inc=sT)
                if ccg == 3:
                    p.op(T, lambda e, sub=sub, half=half: e.matmul(
                        ps[:, sub * 1024 + half * 512:sub * 1024 + (half + 1) * 512],
                        sb.c.ones1x128_s[:, :], sb.bstg[:, half * 512:(half + 1) * 512],
                        start=False, stop=True), inc=sT)
        p.wait(V, sT, None)
        if ccg == 0:
            p.wait(V, "st", None)
        for sub in range(NSUB):
            if ccg == 0:
                p.op(V, lambda e, sub=sub: e.tensor_copy(sb.facc[:, sub, :], ps[:, sub * 1024:sub * 1024 + 1024]), inc=sV)
            else:
                p.op(V, lambda e, sub=sub: e.tensor_tensor(sb.facc[:, sub, :], sb.facc[:, sub, :],
                                                           ps[:, sub * 1024:sub * 1024 + 1024], ALU.add), inc=sV)
    if FFN_DBG:
        t_ = sb.nc.dram_tensor("dbg_facc_t%d" % tile, [128, NSUB * 1024], F32, kind="ExternalOutput").ap()
        p.wait(SY, sV, None)
        p.op(SY, lambda e, t_=t_: e.dma_start(
            out=t_[:, :], in_=sb.facc[:, :, :].rearrange("p a t -> p (a t)")), inc="st", inc_n=16)
        p.wait(S, "st", None)
        p.wait(V, "st", None)
    p.wait(S, sT, None)
    p.wait(S, sV, None)
    for sub in range(NSUB):
        base = sub * 1024
        p.wait(S, sV, None)
        p.op(S, lambda e, sub=sub: e.activation(sb.scr1[:, :], sb.facc[:, sub, :], AF.Square,
                                                accum_out=sb.st4[:, 0:1]), inc=sS)
        p.op(S, lambda e, sub=sub: e.activation(sb.scr1[:, :], sb.facc[:, sub, :], AF.Identity,
                                                accum_out=sb.st4[:, 1:2]), inc=sS)
        p.wait(V, sS, None)
        p.op(V, lambda e: e.tensor_scalar(sb.st4[:, 2:3], sb.st4[:, 1:2], 1.0 / DM, None, ALU.mult), inc=sV)
        p.op(V, lambda e: e.tensor_scalar(sb.st4[:, 3:4], sb.st4[:, 2:3], sb.st4[:, 2:3], None, ALU.mult), inc=sV)
        p.op(V, lambda e: e.scalar_tensor_tensor(sb.st4[:, 3:4], sb.st4[:, 0:1], 1.0 / DM,
                                                 sb.st4[:, 3:4], ALU.mult, ALU.subtract), inc=sV)
        p.wait(S, sV, None)
        p.op(S, lambda e: e.activation(sb.st4[:, 3:4], sb.st4[:, 3:4], AF.Ln, bias=1e-5), inc=sS)
        p.op(S, lambda e: e.activation(sb.st4[:, 3:4], sb.st4[:, 3:4], AF.Exp, scale=-0.5), inc=sS)
        p.wait(V, sS, None)
        p.op(V, lambda e, sub=sub: e.tensor_scalar(sb.scr1[:, :], sb.facc[:, sub, :],
                                                   sb.st4[:, 2:3], sb.st4[:, 3:4],
                                                   ALU.subtract, ALU.mult), inc=sV)
        p.op(V, lambda e, sub=sub: e.tensor_tensor(sb.outb[:, sub, :], sb.scr1[:, :], sb.xm[:, sub, :], ALU.add), inc=sV)


def build_full(NT_DUP, NT_OWN, TT=512):
    NT = NT_DUP + NT_OWN
    NCH = TT // C
    NSUB = TT // 128
    L = NT_OWN * TT
    nc = bass.Bass("TRN2", target_bir_lowering=False, debug=False)
    cns = make_consts(nc)
    d = NS()
    d.x_all = nc.dram_tensor("x_all", [NT * TT, DM], F32, kind="ExternalInput").ap()
    d.dup_mask = nc.dram_tensor("dup_mask", [1, 1], F32, kind="ExternalInput").ap()
    for nm in ("WqT", "WkT", "WvT", "WoT"):
        setattr(d, nm, nc.dram_tensor(nm, [DM, DM], F32, kind="ExternalInput").ap())
    d.WgbT = nc.dram_tensor("WgbT", [DM, 32], F32, kind="ExternalInput").ap()
    d.convw = nc.dram_tensor("convw", [DM, 12], F32, kind="ExternalInput").ap()
    d.convb = nc.dram_tensor("convb", [DM, 3], F32, kind="ExternalInput").ap()
    d.w_fc = nc.dram_tensor("w_fc", [HID, DM], F32, kind="ExternalInput").ap()
    d.w_proj = nc.dram_tensor("w_proj", [DM, HID], F32, kind="ExternalInput").ap()
    d.b_fc = nc.dram_tensor("b_fc", [1, HID], F32, kind="ExternalInput").ap()
    d.b_proj = nc.dram_tensor("b_proj", [1, DM], F32, kind="ExternalInput").ap()
    d.out = nc.dram_tensor("out", [L, DM], F32, kind="ExternalOutput").ap()
    d.xmid = nc.dram_tensor("xmid_scr", [L, DM], F32, kind=("ExternalOutput" if XMID_OUT else "Internal")).ap()
    d.wfcq = nc.dram_tensor("wfcq_scr", [DM, HID], BF16, kind=("ExternalOutput" if XMID_OUT else "Internal")).ap()
    d.wprq = nc.dram_tensor("wprq_scr", [HID, DM], BF16, kind=("ExternalOutput" if XMID_OUT else "Internal")).ap()
    d.wvq = nc.dram_tensor("wvq_scr", [DM, DM], BF16, kind="Internal").ap()
    d.woq = nc.dram_tensor("woq_scr", [DM, DM], BF16, kind="Internal").ap()
    d.wkq = nc.dram_tensor("wkq_scr", [DM, DM], BF16, kind="Internal").ap()
    d.wqq = nc.dram_tensor("wqq_scr", [DM, DM], BF16, kind="Internal").ap()

    p = Plan()
    sW = "w"
    with contextlib.ExitStack() as st:
        sb = NS()
        sb.nc = nc
        sb.c = cns
        al = lambda name, shape, dt: st.enter_context(nc.sbuf_tensor(name, shape, dt))
        sb.ps = st.enter_context(nc.psum_tensor("ps", [128, 4096], F32))
        for nm in (sT, sV, sS, sG, sL, sW, "st", "sb"):
            p.sems[nm] = st.enter_context(nc.semaphore("sem_" + nm))
        stage_consts(nc, p, sb, st)
        sb.wgb_sb = al("wgb_sb", [128, 8, 48], BF16)
        sb.convw_sb = al("convw_sb", [128, 8, 12], F32)
        sb.convb_sb = al("convb_sb", [128, 8, 3], F32)
        sb.wv_buf = al("wv_buf", [128, 8, 1024], BF16)
        sb.x = al("x_sb", [128, NSUB, 1024], F32)
        sb.xm = al("xm_sb", [128, NSUB, 1024], F32)
        sb.scr1 = al("scr1", [128, 1024], F32)
        sb.st4 = al("st4", [128, 4], F32)
        sb.S32 = al("S32", [128, 8, 64], F32)
        sb.Sbf = al("Sbf", [128, 8, 64], BF16)
        sb.maskrep = al("maskrep", [128, 1], F32)
        sb.mask_sb = al("mask_sb", [1, 1], F32)
        sb.t16 = al("t16", [16, 128], F32)
        sb.t16b = al("t16b", [16, 128], F32)
        sb.t16c = al("t16c", [16, TT], F32)
        sb.t16d = al("t16d", [16, TT], F32)
        sb.gT = al("gT", [16, TT], F32)
        sb.gcT = al("gcT", [16, TT], F32)
        sb.nbpT = al("nbpT", [16, TT], F32)
        sb.betaT = al("betaT", [16, TT], F32)
        sb.gmbT = al("gmbT", [16, TT], F32)
        sb.COLS = al("COLS", [128, 80], F32)
        sb.qs = al("qs", [128, 80], F32)
        sb.qi = al("qi", [128, 48], I32)
        sb.qm = al("qm", [128, 512], F32)
        sb.qr = al("qr", [128, 512], I32)
        sb.qd = al("qd", [128, 512], F32)
        sb.qf = al("qf", [128, 512], F32)

        with contextlib.ExitStack() as stA:
            alA = lambda name, shape, dt: stA.enter_context(nc.sbuf_tensor(name, shape, dt))
            wstg = alA("wstg", [128, 4096], F32)
            wbstg = alA("wbstg", [128, 4096], BF16)
            wtstg = alA("wtstg", [128, 4096], BF16)
            p.op(SY, lambda e: e.dma_start(out=sb.mask_sb[:, :], in_=d.dup_mask[:, :]), inc=sL, inc_n=16)
            p.wait(T, sL, None)
            p.op(T, lambda e: e.matmul(sb.ps[:, 3584:3585], sb.c.ones1x128_s[:, :], sb.mask_sb[:, :],
                                       start=True, stop=True), inc=sT)
            p.wait(V, sT, None)
            p.op(V, lambda e: e.tensor_copy(sb.maskrep[:, :], sb.ps[:, 3584:3585]), inc=sV)
            p.op(V, lambda e: e.memset(sb.S32[:, :, :], 0.0), inc=sV)
            p.op(V, lambda e: e.memset(sb.Sbf[:, :, :], 0.0), inc=sV)
            for (dram, dq) in ((d.WqT, d.wqq), (d.WkT, d.wkq), (d.WvT, d.wvq), (d.WoT, d.woq)):
                for dc in range(8):
                    p.wait(SY, sV, None)
                    p.wait(SY, "st", None)
                    p.op(SY, lambda e, dram=dram, dc=dc: e.dma_start(
                        out=wstg[:, 0:1024], in_=dram[dc * 128:(dc + 1) * 128, :]), inc=sL, inc_n=16)
                    p.wait(V, sL, None)
                    p.op(V, lambda e, dc=dc: e.tensor_copy(wbstg[:, 0:1024], wstg[:, 0:1024]), inc=sV)
                    p.wait(SY, sV, None)
                    p.op(SY, lambda e, dq=dq, dc=dc: e.dma_start(
                        out=dq[dc * 128:(dc + 1) * 128, :], in_=wbstg[:, 0:1024]), inc="st", inc_n=16)
            p.wait(SY, sV, None)
            p.op(SY, lambda e: e.dma_start(out=wstg[:, 0:256].rearrange("p (a w) -> p a w", w=32),
                                           in_=d.WgbT.rearrange("(a p) w -> p a w", p=128)), inc=sL, inc_n=16)
            p.op(SY, lambda e: e.dma_start(out=sb.convw_sb[:, :, :],
                                           in_=d.convw.rearrange("(a p) w -> p a w", p=128)), inc=sL, inc_n=16)
            p.op(SY, lambda e: e.dma_start(out=sb.convb_sb[:, :, :],
                                           in_=d.convb.rearrange("(a p) w -> p a w", p=128)), inc=sL, inc_n=16)
            p.wait(V, sL, None)
            p.op(V, lambda e: e.memset(sb.wgb_sb[:, :, :], 0.0), inc=sV)
            p.op(V, lambda e: e.tensor_copy(sb.wgb_sb[:, :, 0:16],
                                            wstg[:, 0:256].rearrange("p (a w) -> p a w", w=32)[:, :, 0:16]), inc=sV)
            p.op(V, lambda e: e.tensor_copy(sb.wgb_sb[:, :, 32:48],
                                            wstg[:, 0:256].rearrange("p (a w) -> p a w", w=32)[:, :, 16:32]), inc=sV)
            p.op(V, lambda e: e.tensor_scalar(sb.convw_sb[:, :, 8:12], sb.convw_sb[:, :, 8:12],
                                              float(Kd) ** -0.5, None, ALU.mult), inc=sV)
            p.op(V, lambda e: e.tensor_scalar(sb.convb_sb[:, :, 2:3], sb.convb_sb[:, :, 2:3],
                                              float(Kd) ** -0.5, None, ALU.mult), inc=sV)
            def prep_wmat(dram, dqout, nrows, ncols, tag):
                nslab = nrows // 128
                for si in range(nslab):
                    p.wait(SY, sV, None)
                    p.wait(SY, sT, None)
                    p.op(SY, lambda e, si=si, dram=dram, ncols=ncols: e.dma_start(
                        out=wstg[:, 0:ncols], in_=dram[si * 128:(si + 1) * 128, :]), inc=sL, inc_n=16)
                    p.wait(V, sL, None)
                    p.wait(V, "st", None)
                    emit_quant(p, sb, lambda i: wstg[:, i * 512:(i + 1) * 512],
                               lambda i: wbstg[:, i * 512:(i + 1) * 512], ncols // 512)
                    qv = p.now(sV)
                    p.wait(T, sV, qv)
                    p.wait(T, sS, None)
                    p.wait(T, "st", None)
                    nch2 = ncols // 128
                    for ch in range(nch2):
                        bank = (ch % 4) * 128
                        p.op(T, lambda e, ch=ch, bank=bank: e.matmul(sb.ps[:, 2048 + bank:2048 + bank + 128],
                            lhsT=wbstg[:, ch * 128:(ch + 1) * 128], rhs=sb.c.identb_s[:, :], start=True, stop=True), inc=sT)
                        if ch % 4 == 3:
                            p.wait(V, sT, None)
                            p.op(V, lambda e, ch=ch: e.tensor_copy(
                                wtstg[:, (ch - 3) * 128:(ch + 1) * 128].rearrange("p (a t) -> p a t", a=4),
                                sb.ps[:, 2048:2048 + 512].rearrange("p (a t) -> p a t", a=4)), inc=sV)
                            p.wait(T, sV, None)
                    p.wait(SY, sV, None)
                    p.wait(SY, "st", None)
                    def f(e, si=si, dqout=dqout, nch2=nch2):
                        return e.dma_start(
                            out=dqout.rearrange("(cpart q) dd -> q cpart dd", q=128)[:, :, si * 128:(si + 1) * 128],
                            in_=wtstg[:, 0:nch2 * 128].rearrange("p (cpart t) -> p cpart t", t=128))
                    p.op(SY, f, inc="st", inc_n=16)
            prep_wmat(d.w_fc, d.wfcq, HID, DM, "fc")
            prep_wmat(d.w_proj, d.wprq, DM, HID, "pr")
        for _eng in (V, S, T, G):
            p.wait(_eng, "st", None)
            p.wait(_eng, sL, None)

        with contextlib.ExitStack() as stB:
            alB = lambda name, shape, dt: stB.enter_context(nc.sbuf_tensor(name, shape, dt))
            sb.xn_bf = alB("xn_bf", [128, 1024], BF16)
            sb.xnT = alB("xnT", [128, 8, TT], BF16)
            sb.raw_oc = alB("raw_oc", [128, TT + 3], F32)
            sb.halo = alB("halo", [128, 3, 8, 3], F32)
            sb.cv1 = alB("cv1", [128, TT], F32)
            sb.cv2 = alB("cv2", [128, TT], F32)
            sb.cvv = alB("cvv", [128, TT], BF16)
            sb.kqT = alB("kqT_s", [128, 8, 2, TT], BF16)
            sb.ktok = alB("ktok_s", [128, NSUB, 1024], BF16)
            sb.vtok = alB("vtok_s", [128, NSUB, 1024], BF16)
            sb.strip1 = alB("strip1", [128, 1024], BF16)
            sb.strip2 = alB("strip2", [128, 1024], BF16)
            sb.strip3 = alB("strip3", [128, 1024], BF16)
            sb.pw = alB("pw", [128, 10, 16, 64], BF16)
            sb.pwp = alB("pwp", [128, 5, 16, 64], BF16)
            sb.AqkT = alB("AqkT", [128, 16, 64], BF16)
            sb.QtT = alB("QtT", [128, 8, 128], BF16)
            sb.X = alB("X", [128, 2, 16, 64], BF16)
            sb.u = alB("u", [128, 16, 64], BF16)
            sb.bv = alB("bv", [128, 1024], BF16)
            sb.kbar = alB("kbar", [128, 1024], BF16)
            sb.oT = alB("oT_s", [128, 8, TT], BF16)

            for tile in range(NT):
                is_dup = tile < NT_DUP
                tsl = slice(tile * TT, (tile + 1) * TT)
                p.wait(SY, sV, None)
                p.op(SY, lambda e, tsl=tsl: e.dma_start(
                    out=sb.x[:, :, :], in_=d.x_all[tsl, :].rearrange("(s p) dd -> p s dd", p=128)), inc=sL, inc_n=16)
                p.wait(SY, sT, None)
                p.op(SY, lambda e: e.dma_start(
                    out=sb.wv_buf[:, :, :], in_=d.wkq.rearrange("(a p) c -> p a c", p=128)), inc=sL, inc_n=16)
                if tile == NT_DUP and NT_DUP > 0:
                    p.wait(V, sT, None)
                    mrep = sb.maskrep[:, 0:1].unsqueeze(1).broadcast_to([128, 8, 64])
                    p.op(V, lambda e, mrep=mrep: e.tensor_tensor(sb.S32[:, :, :], sb.S32[:, :, :], mrep, ALU.mult), inc=sV)
                    p.op(V, lambda e: e.tensor_copy(sb.Sbf[:, :, :], sb.S32[:, :, :]), inc=sV)
                emit_attn_front(p, sb, d, tile, TT, is_dup, tile == 0, tile == NT_DUP and NT_DUP > 0, last_dup=(tile == NT_DUP - 1))
                if DBG:
                    def dump(nm, h, tile=tile):
                        shp = list(h.shape)
                        flat = 1
                        for sdim in shp[1:]:
                            flat *= sdim
                        pats = {2: "p a -> p (a)", 3: "p a b -> p (a b)", 4: "p a b t -> p (a b t)"}
                        src = h[tuple([slice(None)] * len(shp))]
                        if len(shp) > 2:
                            src = src.rearrange(pats[len(shp)])
                        t = nc.dram_tensor("dbg_%s_t%d" % (nm, tile), [shp[0], flat], h.dtype, kind="Internal").ap()
                        p.op(SY, lambda e, t=t, src=src: e.dma_start(out=t[:, :], in_=src), inc="st", inc_n=16)
                    for _s in (sV, sS, sG, sT):
                        p.wait(SY, _s, None)
                    for nm in ("kqT", "vtok", "ktok", "gT", "gcT", "betaT", "nbpT", "gmbT", "xnT"):
                        dump(nm, getattr(sb, nm))
                    for _e in (V, S, T, G):
                        p.wait(_e, "st", None)
                emit_mixer_tile(p, sb, cns, 0, NCH, TT, is_dup)
                if DBG:
                    for _s in (sV, sS, sG, sT):
                        p.wait(SY, _s, None)
                    for nm in (("oT", "S32", "COLS") if not is_dup else ("S32", "COLS")):
                        dump(nm, getattr(sb, nm))
                    for _e in (V, S, T, G):
                        p.wait(_e, "st", None)
                if not is_dup:
                    p.wait(SY, sT, None)
                    p.op(SY, lambda e: e.dma_start(
                        out=sb.wv_buf[:, :, :], in_=d.woq.rearrange("(a p) c -> p a c", p=128)), inc=sL, inc_n=16)
                    emit_oproj(p, sb, tile, TT)
                    osl = slice((tile - NT_DUP) * TT, (tile - NT_DUP + 1) * TT)
                    p.wait(SY, sV, None)
                    p.op(SY, lambda e, osl=osl: e.dma_start(
                        out=d.xmid[osl, :].rearrange("(s p) dd -> p s dd", p=128), in_=sb.xm[:, :, :]), inc="st", inc_n=16)

        with contextlib.ExitStack() as stC:
            alC = lambda name, shape, dt: stC.enter_context(nc.sbuf_tensor(name, shape, dt))
            sb.xn2 = alC("xn2", [128, 1024], F32)
            sb.xq = alC("xq", [128, 1024], BF16)
            sb.xqT = alC("xqT", [128, 8, TT], BF16)
            sb.h32 = alC("h32", [128, 512], F32)
            sb.hqt_tmp = alC("hqt_tmp", [128, 512], BF16)
            sb.hqT = alC("hqT", [128, NSUB, 32, 128], BF16)
            sb.wfc_buf = alC("wfc_buf", [128, 8, 512], BF16)
            sb.bstg = alC("bstg", [1, HID], F32)
            sb.wpr_buf = alC("wpr_buf", [128, 8, 1024], BF16)
            sb.facc = alC("facc", [128, NSUB, 1024], F32)
            sb.outb = sb.facc
            sWF = sW
            p.op(SY, lambda e: e.dma_start(out=sb.bstg[:, :], in_=d.b_fc[:, :]), inc="sb", inc_n=16)
            p.wait(T, "sb", None)
            for tile in range(NT_OWN):
                osl = slice(tile * TT, (tile + 1) * TT)
                p.wait(SY, sV, None)
                p.wait(SY, "st", None)
                p.op(SY, lambda e, osl=osl: e.dma_start(
                    out=sb.xm[:, :, :], in_=d.xmid[osl, :].rearrange("(s p) dd -> p s dd", p=128)), inc=sL, inc_n=16)
                emit_ffn_tile(p, sb, d, tile, TT, sWF)
                p.wait(SY, sV, None)
                p.op(SY, lambda e, osl=osl: e.dma_start(
                    out=d.out[osl, :].rearrange("(s p) dd -> p s dd", p=128), in_=sb.outb[:, :, :]), inc="st", inc_n=16)

        p.wait(G, "st", None)
        p.wait(G, sL, None)
        p.wait(G, sWF, None)
        p.wait(G, "sb", None)

        blk = st.enter_context(nc.Block())
        for eng_name, section in ((SY, blk.sync), (S, blk.scalar), (V, blk.vector), (T, blk.tensor), (G, blk.gpsimd)):
            def run(e, eng_name=eng_name):
                for fn in p.q[eng_name]:
                    fn(e)
            section(run)
    return nc


_NC_CACHE = {}


def _get_nc():
    if "nc" not in _NC_CACHE:
        _NC_CACHE["nc"] = build_full(8, 8, 512)
    return _NC_CACHE["nc"]


def kernel(**inputs):
    from concourse.bass_utils import run_bass_kernel_spmd
    f32 = np.float32
    x = np.asarray(inputs["L0_attn_x"], f32)
    Wq = np.asarray(inputs["L0_attn_Wq"], f32)
    Wk = np.asarray(inputs["L0_attn_Wk"], f32)
    Wv = np.asarray(inputs["L0_attn_Wv"], f32)
    Wo = np.asarray(inputs["L0_attn_Wo"], f32)
    Wg = np.asarray(inputs["L0_attn_Wg"], f32)
    Wb = np.asarray(inputs["L0_attn_Wb"], f32)
    cq_w = np.asarray(inputs["L0_attn_cq_w"], f32)
    ck_w = np.asarray(inputs["L0_attn_ck_w"], f32)
    cv_w = np.asarray(inputs["L0_attn_cv_w"], f32)
    cq_b = np.asarray(inputs["L0_attn_cq_b"], f32)
    ck_b = np.asarray(inputs["L0_attn_ck_b"], f32)
    cv_b = np.asarray(inputs["L0_attn_cv_b"], f32)
    w_fc = np.asarray(inputs["L0_ffn_w_fc"], f32)
    b_fc = np.asarray(inputs["L0_ffn_b_fc"], f32)
    w_proj = np.asarray(inputs["L0_ffn_w_proj"], f32)
    b_proj = np.asarray(inputs["L0_ffn_b_proj"], f32)

    common = {
        "WqT": np.ascontiguousarray(Wq.T),
        "WkT": np.ascontiguousarray(Wk.T),
        "WvT": np.ascontiguousarray(Wv.T),
        "WoT": np.ascontiguousarray(Wo.T),
        "WgbT": np.ascontiguousarray(np.concatenate([Wg.T, Wb.T], 1)),
        "convw": np.ascontiguousarray(
            np.concatenate([ck_w[:, 0, :], cv_w[:, 0, :], cq_w[:, 0, :]], 1)),
        "convb": np.ascontiguousarray(np.stack([ck_b, cv_b, cq_b], 1)),
        "w_fc": w_fc,
        "w_proj": w_proj,
        "b_fc": np.ascontiguousarray(b_fc[None, :]),
        "b_proj": np.ascontiguousarray(b_proj[None, :]),
    }
    SH = 4096
    in_maps = []
    for c in range(8):
        b = c // 2
        if c % 2 == 0:
            x_all = np.concatenate([x[b, 0:SH], x[b, 0:SH]], 0)
            mk = 0.0
        else:
            x_all = np.ascontiguousarray(x[b])
            mk = 1.0
        m = dict(common)
        m["x_all"] = np.ascontiguousarray(x_all)
        m["dup_mask"] = np.array([[mk]], f32)
        in_maps.append(m)

    nc = _get_nc()
    res = run_bass_kernel_spmd(nc, in_maps, core_ids=list(range(8)))
    y = np.empty((4, 8192, 1024), f32)
    for c in range(8):
        b = c // 2
        seg = np.asarray(res.results[c]["out"], f32)
        if c % 2 == 0:
            y[b, 0:SH] = seg
        else:
            y[b, SH:2 * SH] = seg
    return y
